# version 27
# baseline (speedup 1.0000x reference)
import numpy as np
import ml_dtypes
import concourse.bass as bass
import concourse.mybir as mybir
from concourse.bass_utils import run_bass_kernel_spmd

F32 = mybir.dt.float32
F32R = mybir.dt.float32r
BF16 = mybir.dt.bfloat16
F8 = mybir.dt.float8e4
AX = mybir.AxisListType.X
ALU = mybir.AluOpType
ACT = mybir.ActivationFunctionType

D = 1024
H = 2
DK = 512
C = 128
LNW_CLAMP = 0.55
EPS = 1e-5
INV_D = 1.0 / 1024.0

ENGINES = ("sy", "g", "t", "v", "s")


class Sched:

    FULLW = ("expl", "expnl", "ybuf", "ktok", "xlnT", "yt", "xln0", "xln1",
             "x2ln", "vtok0", "vtok1", "vtok2", "vt0", "vt1", "vt2",
             "qtb0", "qtb1", "qtr0", "qtr1", "fC", "fB")

    def __init__(self, reorder=True):
        self.do_reorder = reorder
        self.rawops = {e: [] for e in ENGINES}
        self.ops = {e: [] for e in ENGINES}
        self.n = {e: 0 for e in ENGINES}
        self.w = {}
        self.r = {}
        self.gseq = []

    def op(self, eng, fn, reads=(), writes=(), dma=False, cost=None):
        deps = set()
        odeps = set()
        if dma and self.n[eng]:
            deps.add((eng, self.n[eng] - 1))

        def need(de, di, d=deps):
            d.add((de, di))

        for res in reads:
            lw = self.w.get(res)
            if lw:
                need(*lw)
        for res in writes:
            lw = self.w.get(res)
            if lw:
                de, di = lw
                if not (de == eng == "t" and res.startswith("p")):
                    need(de, di)
                else:
                    need(de, di, odeps)
        for res in writes:
            for rr in self.r.get(res, ()):
                need(*rr)
        me = (eng, self.n[eng])
        self.n[eng] += 1
        for res in reads:
            self.r.setdefault(res, []).append(me)
        for res in writes:
            self.w[res] = me
            self.r[res] = []
        if cost is None:
            if dma:
                cost = 2.4
            elif eng == "t":
                cost = 0.10 if "ident" in reads else 0.30
            elif writes and all(w[:2] in ("sm", "mv", "ss", "sc") for w in writes):
                cost = 0.12
            elif any(w in Sched.FULLW for w in writes):
                cost = 1.05
            else:
                cost = 0.6
        self.gseq.append(me)
        self.ops[eng].append((deps, fn, dma, odeps, cost))

    def reorder(self):
        import heapq
        preds = {}
        succs = {k: [] for k in self.gseq}
        cost = {}
        seqno = {k: i for i, k in enumerate(self.gseq)}
        allow = getattr(Sched, "ALLOW", set(ENGINES))
        for e in ENGINES:
            for i, (deps, fn, dma, odeps, c) in enumerate(self.ops[e]):
                k = (e, i)
                ps = deps | odeps
                if e not in allow and i > 0:
                    ps = ps | {(e, i - 1)}
                preds[k] = list(ps)
                cost[k] = c
                for p in ps:
                    succs[p].append(k)
        rank = {}
        for k in reversed(self.gseq):
            rank[k] = cost[k] + max((rank[s] for s in succs[k]), default=0.0)
        if self.do_reorder:
            npred = {k: len(preds[k]) for k in self.gseq}
            finish = {}
            clock = {e: 0.0 for e in ENGINES}
            ready = {e: [] for e in ENGINES}
            for k in self.gseq:
                if npred[k] == 0:
                    heapq.heappush(ready[k[0]], (-rank[k], seqno[k], k))
            order = {e: [] for e in ENGINES}
            nleft = len(self.gseq)
            while nleft:
                e = min((x for x in ENGINES if ready[x]), key=lambda x: clock[x])
                popped = []
                for _ in range(min(24, len(ready[e]))):
                    popped.append(heapq.heappop(ready[e]))
                best, bkey = None, None
                for it in popped:
                    k = it[2]
                    rt = max((finish[p] for p in preds[k]), default=0.0)
                    key = (0 if rt <= clock[e] + 1e-9 else 1, rt if rt > clock[e] else 0.0, it[0], it[1])
                    if best is None or key < best:
                        best, bkey = key, it
                for it in popped:
                    if it is not bkey:
                        heapq.heappush(ready[e], it)
                k = bkey[2]
                rt = max((finish[p] for p in preds[k]), default=0.0)
                st = max(clock[e], rt)
                finish[k] = st + cost[k]
                clock[e] = finish[k]
                order[e].append(k)
                nleft -= 1
                for sk in succs[k]:
                    npred[sk] -= 1
                    if npred[sk] == 0:
                        heapq.heappush(ready[sk[0]], (-rank[sk], seqno[sk], sk))
            self.est_makespan = max(clock.values())
        else:
            order = {e: [(e, i) for i in range(len(self.ops[e]))] for e in ENGINES}
        newpos = {e: {} for e in ENGINES}
        for e in ENGINES:
            for ni, (_, oi) in enumerate(order[e]):
                newpos[e][oi] = ni
        newops = {e: [] for e in ENGINES}
        for e in ENGINES:
            for _, oi in order[e]:
                deps, fn, dma, odeps, c = self.ops[e][oi]
                nd = {}
                for de, di in deps:
                    ni = newpos[de][di]
                    if nd.get(de, -1) < ni:
                        nd[de] = ni
                for de, ni in list(nd.items()):
                    assert (de != e) or ni < newpos[e][oi]
                newops[e].append((nd, fn, dma))
        self.ops = newops

    def raw(self, eng, fn):
        self.rawops[eng].append(fn)

    def finalize(self):
        self.reorder()
        referenced = {e: set() for e in ENGINES}
        for e in ENGINES:
            for deps, _, _ in self.ops[e]:
                for de, di in deps.items():
                    referenced[de].add(di)
        self.incnum = {e: [] for e in ENGINES}
        self.incs = {e: [] for e in ENGINES}
        self.total = {}
        for e in ENGINES:
            run = 0
            for i, (_, _, is_dma) in enumerate(self.ops[e]):
                amt = 16 if is_dma else (1 if i in referenced[e] else 0)
                run += amt
                self.incs[e].append(amt)
                self.incnum[e].append(run)
            self.total[e] = run

    def emit_engine(self, ename, e, sems):
        for fn in self.rawops[ename]:
            fn(e)
        lastw = {}
        for i, (deps, fn, is_dma) in enumerate(self.ops[ename]):
            for de in ENGINES:
                di = deps.get(de)
                if di is None:
                    continue
                dv = self.incnum[de][di]
                assert self.incs[de][di] > 0, (ename, i, de, di)
                if lastw.get(de, 0) >= dv:
                    continue
                e.wait_ge(sems[de], dv)
                lastw[de] = dv
            inst = fn(e)
            amt = self.incs[ename][i]
            if amt:
                inst.then_inc(sems[ename], amt)
        if ename in ("sy", "g") and self.total[ename]:
            e.wait_ge(sems[ename], self.total[ename])


def build(T=4096, dbg=False):
    nch = T // C
    assert nch % 2 == 0 or nch == 1
    nc = bass.Bass("TRN2", target_bir_lowering=False)

    dx = nc.dram_tensor("x", [T, D], F32, kind="ExternalInput").ap()
    d_wqt = nc.dram_tensor("wqt", [128, 8192], F32, kind="ExternalInput").ap()
    d_wkt = nc.dram_tensor("wkt", [128, 8192], F32, kind="ExternalInput").ap()
    d_wvt = nc.dram_tensor("wvt", [128, 8192], F32, kind="ExternalInput").ap()
    d_wot = nc.dram_tensor("wot", [128, 8192], F32, kind="ExternalInput").ap()
    d_fk = nc.dram_tensor("fk", [128, 8192], BF16, kind="ExternalInput").ap()
    d_bias = nc.dram_tensor("biasrep", [128, D], BF16, kind="ExternalInput").ap()
    d_wg = nc.dram_tensor("wgrep", [128, D], F32, kind="ExternalInput").ap()
    d_bg = nc.dram_tensor("bgrep", [128, D], F32, kind="ExternalInput").ap()
    d_wl = nc.dram_tensor("wlrep", [128, D], F32, kind="ExternalInput").ap()
    d_bl = nc.dram_tensor("blrep", [128, D], F32, kind="ExternalInput").ap()
    d_sin = nc.dram_tensor("sin", [1, 1], F32, kind="ExternalInput").ap()
    d_sk = nc.dram_tensor("sk", [1, 1], F32, kind="ExternalInput").ap()
    d_ngtril = nc.dram_tensor("negtril", [128, 128], F32R, kind="ExternalInput").ap()
    d_ngones = nc.dram_tensor("negones", [128, 128], F32R, kind="ExternalInput").ap()
    d_tmask = nc.dram_tensor("trilmask", [128, 128], BF16, kind="ExternalInput").ap()
    d_ident = nc.dram_tensor("ident", [128, 128], BF16, kind="ExternalInput").ap()
    d_out = nc.dram_tensor("out", [T, D], F32, kind="ExternalOutput").ap()

    dbg_outs = {}
    if dbg:
        for name, shape, dt in [
            ("dxln", [128, D], BF16), ("dktok", [128, D], BF16),
            ("dkhat", [128, D], BF16), ("dlnw", [128, D], F32),
            ("dexpl", [128, D], BF16), ("dkd", [128, D], BF16),
            ("dgm", [128, 256], BF16), ("dy", [128, D], BF16),
            ("dx2", [128, D], F32), ("dstate0", [128, 2048], F32),
            ("dxq", [128, D], F8), ("dqt", [128, D], F32),
        ]:
            dbg_outs[name] = nc.dram_tensor(name, shape, dt, kind="ExternalOutput").ap()

    sb = []

    def sbuf(name, shape, dt):
        t = nc.sbuf_tensor(name, shape, dt)
        sb.append(t)
        return t.__enter__()

    def psum(name, shape):
        t = nc.psum_tensor(name, shape, F32)
        sb.append(t)
        return t.__enter__()

    wqt = sbuf("wqt_s", [128, 8192], BF16)
    wkt = sbuf("wkt_s", [128, 8192], BF16)
    wvt = sbuf("wvt_s", [128, 8192], BF16)
    wot = sbuf("wot_s", [128, 8192], BF16)
    fkq = sbuf("fkq_s", [128, 8192], F8)
    biasr = sbuf("biasr_s", [128, D], BF16)
    wgr = sbuf("wgr_s", [128, D], BF16)
    bgr = sbuf("bgr_s", [128, D], F32)
    wlr = sbuf("wlr_s", [128, D], BF16)
    blr = sbuf("blr_s", [128, D], F32)
    ngtril = sbuf("ngtril_s", [128, 128], F32R)
    ngones = sbuf("ngones_s", [128, 128], F32R)
    tmask = sbuf("tmask_s", [128, 128], BF16)
    ident = sbuf("ident_s", [128, 128], BF16)
    scal = sbuf("scal_s", [128, 8], F32)
    St = [sbuf("state0", [128, 2048], F32R), sbuf("state1", [128, 2048], F32R)]
    xin = [sbuf(f"xin{i}", [128, D], F32) for i in range(3)]
    obuf = [sbuf(f"obuf{i}", [128, D], F32) for i in range(2)]
    qtr = [sbuf(f"qtr{i}", [128, D], F32R) for i in range(2)]
    qtb = [sbuf(f"qtb{i}", [128, D], BF16) for i in range(2)]
    vtok = [sbuf(f"vtok{i}", [128, D], BF16) for i in range(3)]
    vt = [sbuf(f"vt{i}", [128, D], BF16) for i in range(3)]
    lnw = [sbuf(f"lnw{i}", [128, D], F32R) for i in range(3)]
    k2 = [sbuf(f"k2_{i}", [128, D], BF16) for i in range(2)]
    kd = [sbuf(f"kd{i}", [128, D], BF16) for i in range(2)]
    xln = [sbuf(f"xln{i}", [128, D], BF16) for i in range(2)]
    xlnT = sbuf("xlnT", [128, D], BF16)
    ktok = sbuf("ktok", [128, D], BF16)
    khat = sbuf("khat", [128, D], BF16)
    fA = sbuf("fA", [128, D], F32)
    fB = sbuf("fB", [128, D], F32)
    fC = sbuf("fC", [128, D], F32)
    rho = sbuf("rho", [128, D], BF16)
    expl = sbuf("expl", [128, D], BF16)
    expnl = sbuf("expnl", [128, D], BF16)
    gm = sbuf("gm", [128, 256], BF16)
    gmx = sbuf("gmx", [128, 256], BF16)
    ybuf = sbuf("ybuf", [128, D], BF16)
    yt = sbuf("yt", [128, D], BF16)
    x2 = sbuf("x2buf", [128, D], F32)
    x2ln = sbuf("x2ln", [128, D], BF16)
    xqt = sbuf("xqt", [128, D], F8)
    mv = sbuf("mv", [128, 2], F32)
    smallA = sbuf("smallA", [128, 24], F32)
    ssb = sbuf("ssb", [128, 2], F32)
    rk = sbuf("rk", [128, 2], F32)
    pA = psum("pA", [128, 1024])
    pB = psum("pB", [128, 1024])
    pY = psum("pY", [128, 1024])
    pL = psum("pL", [128, 1024])

    S = Sched()

    def _load_lnexp_table(e):
        return e.add_instruction(mybir.InstLoadActFuncSet(
            name=e.bass.get_next_instruction_name(), act_func_set_id=6, ins=[], outs=[]))
    S.raw("s", _load_lnexp_table)
    S.op("sy", lambda e: e.dma_start(xin[0][:], dx[0:128, :]), writes=["xin0"], dma=True)
    if nch > 1:
        S.op("sy", lambda e: e.dma_start(xin[1][:], dx[128:256, :]), writes=["xin1"], dma=True)
    S.op("sy", lambda e: e.dma_start(ngtril[:], d_ngtril), writes=["ngtril"], dma=True)
    S.op("sy", lambda e: e.dma_start(ngones[:], d_ngones), writes=["ngones"], dma=True)
    S.op("sy", lambda e: e.dma_start(tmask[:], d_tmask), writes=["tmask"], dma=True)
    S.op("sy", lambda e: e.dma_start(ident[:], d_ident), writes=["ident"], dma=True)
    S.op("sy", lambda e: e.dma_start(scal[:, 0:1], bass.AP(d_sin.tensor, 0, [[0, 128], [1, 1]])), writes=["scal01"], dma=True)
    S.op("sy", lambda e: e.dma_start(scal[:, 1:2], bass.AP(d_sk.tensor, 0, [[0, 128], [1, 1]])), writes=["scal01"], dma=True)
    S.op("sy", lambda e: e.dma_start(bgr[:], d_bg), writes=["bgr"], dma=True)
    S.op("sy", lambda e: e.dma_start(blr[:], d_bl), writes=["blr"], dma=True)
    S.op("sy", lambda e: e.dma_start(biasr[:], d_bias), writes=["biasr"], dma=True)
    S.op("sy", lambda e: e.dma_start(wqt[:], d_fk), writes=["wqt"], dma=True)
    S.op("v", lambda e: e.reciprocal(scal[:, 2:3], scal[:, 0:1]), reads=["scal01"], writes=["scal2"])
    S.op("v", lambda e: e.reciprocal(scal[:, 3:4], scal[:, 1:2]), reads=["scal01"], writes=["scal3"])
    S.op("v", lambda e: e.tensor_tensor(scal[:, 4:5], scal[:, 0:1], scal[:, 1:2], ALU.mult), reads=["scal01"], writes=["scal4"])
    S.op("v", lambda e: e.tensor_scalar(fkq[:], wqt[:], scal[:, 3:4], None, ALU.mult), reads=["wqt", "scal3"], writes=["fkq"])
    S.op("g", lambda e: e.dma_start(wgr[:], d_wg), writes=["wgr"], dma=True)
    S.op("g", lambda e: e.dma_start(wlr[:], d_wl), writes=["wlr"], dma=True)
    S.op("g", lambda e: e.dma_start(wkt[:], d_wkt), writes=["wkt"], dma=True)
    S.op("g", lambda e: e.dma_start(wqt[:], d_wqt), writes=["wqt"], dma=True)
    S.op("g", lambda e: e.dma_start(wvt[:], d_wvt), writes=["wvt"], dma=True)
    S.op("g", lambda e: e.dma_start(wot[:], d_wot), writes=["wot"], dma=True)
    S.op("v", lambda e: e.memset(St[0][:].bitcast(F32), 0.0), writes=["S0"])
    S.op("v", lambda e: e.memset(St[1][:].bitcast(F32), 0.0), writes=["S1"])

    def stage_a1(c):
        bi = c % 2
        xb = f"xin{c % 3}"
        xt = xin[c % 3]
        if c >= 2:
            S.op("sy", lambda e, c=c, xt=xt: e.dma_start(xt[:], dx[c * 128:(c + 1) * 128, :]), writes=[xb], dma=True)
        S.op("v", lambda e: e.reduce_sum(mv[:, 0:1], xt[:], AX), reads=[xb], writes=["mv"])
        S.op("s", lambda e: e.activation(fB[:], xt[:], ACT.Square, accum_out=mv[:, 1:2]), reads=[xb], writes=["fB", "fBl", "fBh", "mv"])
        S.op("v", lambda e: e.tensor_scalar(smallA[:, 0:1], mv[:, 0:1], INV_D, None, ALU.mult), reads=["mv"], writes=["sm0"])
        S.op("v", lambda e: e.tensor_scalar(smallA[:, 1:2], smallA[:, 0:1], smallA[:, 0:1], -EPS, ALU.mult, ALU.add), reads=["sm0"], writes=["sm1"])
        S.op("v", lambda e: e.scalar_tensor_tensor(smallA[:, 2:3], mv[:, 1:2], INV_D, smallA[:, 1:2], ALU.mult, ALU.subtract), reads=["mv", "sm1"], writes=["sm2"])
        S.op("s", lambda e: e.activation(smallA[:, 3:4], smallA[:, 2:3], ACT.Ln), reads=["sm2"], writes=["sm3"])
        S.op("s", lambda e: e.activation(smallA[:, 4:5], smallA[:, 3:4], ACT.Exp, scale=-0.5), reads=["sm3"], writes=["sm4"])
        S.op("v", lambda e: e.scalar_tensor_tensor(smallA[:, 5:6], smallA[:, 0:1], -1.0, smallA[:, 4:5], ALU.mult, ALU.mult), reads=["sm0", "sm4"], writes=["sm5"])
        S.op("s", lambda e, bi=bi: e.activation(xln[bi][:, 0:512], xt[:, 0:512], ACT.Identity, bias=smallA[:, 5:6], scale=smallA[:, 4:5]), reads=[xb, "sm4", "sm5"], writes=[f"xln{bi}", f"xln{bi}l"], cost=0.55)
        S.op("s", lambda e, bi=bi: e.activation(xln[bi][:, 512:1024], xt[:, 512:1024], ACT.Identity, bias=smallA[:, 5:6], scale=smallA[:, 4:5]), reads=[xb, "sm4", "sm5"], writes=[f"xln{bi}", f"xln{bi}h"], cost=0.55)

    def stage_a2(c, parts=(1, 2, 3, 4, 5, 6, 7)):
        bi = c % 2
        ti = c % 3
        xr = f"xln{bi}"
        xv = xln[bi]
        fns = {1: stage_a2_p1, 2: stage_a2_p2, 3: stage_a2_p3, 4: stage_a2_p4a,
               5: stage_a2_p4b, 6: stage_a2_p4c, 7: stage_a2_p4d}
        for p in parts:
            fns[p](c, bi, ti, xr, xv)

    def stage_a2_p1(c, bi, ti, xr, xv):
        for jq in range(8):
            S.op("t", lambda e, jq=jq, xv=xv: e.matmul(pB[:, jq * 128:(jq + 1) * 128], xv[:, jq * 128:(jq + 1) * 128], ident[:], start=True, stop=True), reads=[xr + ("l" if jq < 4 else "h"), "ident"], writes=["pB"])
        S.op("s", lambda e: e.copy(xlnT[:, 0:512], pB[:, 0:512]), reads=["pB"], writes=["xlnT", "xlnTl"], cost=0.55)
        S.op("s", lambda e: e.copy(xlnT[:, 512:1024], pB[:, 512:1024]), reads=["pB"], writes=["xlnT", "xlnTh"], cost=0.55)

        for nh in range(2):
            for j in range(8):
                S.op("t", lambda e, nh=nh, j=j: e.matmul(pA[:, nh * 512:nh * 512 + 512], xlnT[:, j * 128:j * 128 + 128], wkt[:, j * 1024 + nh * 512: j * 1024 + nh * 512 + 512], start=(j == 0), stop=(j == 7)), reads=["xlnT" + ("l" if j < 4 else "h"), "wkt"], writes=["pA"])
        S.op("s", lambda e: e.copy(ktok[:], pA[:]), reads=["pA"], writes=["ktok", "ktokl", "ktokh"])

        for h in range(2):
            S.op("s", lambda e, h=h: e.activation(fA[:, h * 512:h * 512 + 512], ktok[:, h * 512:h * 512 + 512], ACT.Square, accum_out=ssb[:, h:h + 1]), reads=["ktok"], writes=["fA", "fAl", "fAh", "ssb"])
        S.op("v", lambda e: e.tensor_scalar(ssb[:], ssb[:], 1e-24, None, ALU.max), reads=["ssb"], writes=["ssb"])
        S.op("s", lambda e: e.activation(smallA[:, 6:8], ssb[:], ACT.Ln), reads=["ssb"], writes=["sm67"])
        S.op("s", lambda e: e.activation(rk[:], smallA[:, 6:8], ACT.Exp, scale=-0.5), reads=["sm67"], writes=["rk"])
        for h in range(2):
            S.op("v", lambda e, h=h: e.tensor_scalar(khat[:, h * 512:h * 512 + 512], ktok[:, h * 512:h * 512 + 512], rk[:, h:h + 1], None, ALU.mult), reads=["ktok", "rk"], writes=["khat", "khat" + ("l" if h == 0 else "h")])

    def stage_a2_p2(c, bi, ti, xr, xv):
        for nh in range(2):
            for j in range(8):
                S.op("t", lambda e, nh=nh, j=j: e.matmul(pA[:, nh * 512:nh * 512 + 512], xlnT[:, j * 128:j * 128 + 128], wqt[:, j * 1024 + nh * 512: j * 1024 + nh * 512 + 512], start=(j == 0), stop=(j == 7)), reads=["xlnT" + ("l" if j < 4 else "h"), "wqt"], writes=["pA"])
        S.op("s", lambda e, xv=xv: e.copy(xv[:, 0:512], pA[:, 0:512]), reads=["pA"], writes=[xr, xr + "l"], cost=0.55)
        S.op("s", lambda e, xv=xv: e.copy(xv[:, 512:1024], pA[:, 512:1024]), reads=["pA"], writes=[xr, xr + "h"], cost=0.55)
        for jq in range(8):
            S.op("t", lambda e, jq=jq, xv=xv: e.matmul(pB[:, jq * 128:(jq + 1) * 128], xv[:, jq * 128:(jq + 1) * 128], ident[:], start=True, stop=True), reads=[xr + ("l" if jq < 4 else "h"), "ident"], writes=["pB"])
        S.op("s", lambda e, bi=bi: e.copy(qtr[bi][:], pB[:]), reads=["pB"], writes=[f"qtr{bi}"])
        S.op("s", lambda e, bi=bi: e.copy(qtb[bi][:], pB[:]), reads=["pB"], writes=[f"qtb{bi}"])

    def stage_a2_p3(c, bi, ti, xr, xv):
        for nh in range(2):
            for j in range(8):
                S.op("t", lambda e, nh=nh, j=j: e.matmul(pA[:, nh * 512:nh * 512 + 512], xlnT[:, j * 128:j * 128 + 128], wvt[:, j * 1024 + nh * 512: j * 1024 + nh * 512 + 512], start=(j == 0), stop=(j == 7)), reads=["xlnT" + ("l" if j < 4 else "h"), "wvt"], writes=["pA"])
        for q in range(4):
            if q % 2 == 0:
                S.op("s", lambda e, ti=ti, q=q: e.copy(vtok[ti][:, q * 256:q * 256 + 256], pA[:, q * 256:q * 256 + 256]), reads=["pA"], writes=[f"vtok{ti}", f"vtok{ti}q{q}"], cost=0.3)
            else:
                S.op("v", lambda e, ti=ti, q=q: e.tensor_copy(vtok[ti][:, q * 256:q * 256 + 256], pA[:, q * 256:q * 256 + 256]), reads=["pA"], writes=[f"vtok{ti}", f"vtok{ti}q{q}"], cost=0.3)
        for jq in range(8):
            S.op("t", lambda e, jq=jq, ti=ti: e.matmul(pB[:, jq * 128:(jq + 1) * 128], vtok[ti][:, jq * 128:(jq + 1) * 128], ident[:], start=True, stop=True), reads=[f"vtok{ti}q{jq // 2}", "ident"], writes=["pB"])
        S.op("s", lambda e, ti=ti: e.copy(vt[ti][:], pB[:]), reads=["pB"], writes=[f"vt{ti}"])

    HS = [(slice(0, 512), "l"), (slice(512, 1024), "h")]

    def stage_a2_p4a(c, bi, ti, xr, xv):
        for sl, u in HS:
            S.op("v", lambda e, sl=sl: e.tensor_tensor(rho[:, sl], khat[:, sl], wgr[:, sl], ALU.mult), reads=[f"khat{u}", "wgr"], writes=["rho", f"rho{u}"])
        for sl, u in HS:
            S.op("v", lambda e, sl=sl: e.tensor_tensor(fB[:, sl], rho[:, sl], bgr[:, sl], ALU.add), reads=[f"rho{u}", "bgr"], writes=["fB", f"fB{u}"])
        for sl, u in HS:
            S.op("s", lambda e, sl=sl: e.activation(fA[:, sl], fB[:, sl], ACT.Exp, scale=-1.0), reads=[f"fB{u}"], writes=["fA", f"fA{u}"])

    def stage_a2_p4b(c, bi, ti, xr, xv):
        for sl, u in HS:
            S.op("v", lambda e, sl=sl: e.tensor_tensor(rho[:, sl], khat[:, sl], wlr[:, sl], ALU.mult), reads=[f"khat{u}", "wlr"], writes=["rho", f"rho{u}"])
        for sl, u in HS:
            S.op("v", lambda e, sl=sl: e.tensor_tensor(fB[:, sl], rho[:, sl], blr[:, sl], ALU.add), reads=[f"rho{u}", "blr"], writes=["fB", f"fB{u}"])
        for sl, u in HS:
            S.op("s", lambda e, sl=sl: e.activation(fC[:, sl], fB[:, sl], ACT.Exp), reads=[f"fB{u}"], writes=["fC", f"fC{u}"])

    def stage_a2_p4c(c, bi, ti, xr, xv):
        for sl, u in HS:
            S.op("v", lambda e, sl=sl: e.tensor_scalar(fB[:, sl], fC[:, sl], 1.0, None, ALU.add), reads=[f"fC{u}"], writes=["fB", f"fB{u}"])
        for sl, u in HS:
            S.op("v", lambda e, sl=sl: e.scalar_tensor_tensor(fC[:, sl], fA[:, sl], 1.0, fB[:, sl], ALU.add, ALU.mult), reads=[f"fA{u}", f"fB{u}"], writes=["fC", f"fC{u}"])
        for sl, u in HS:
            S.op("v", lambda e, sl=sl: e.tensor_scalar(fA[:, sl], fC[:, sl], float(np.exp(np.float32(LNW_CLAMP))), None, ALU.min), reads=[f"fC{u}"], writes=["fA", f"fA{u}"])
        for sl, u in HS:
            S.op("s", lambda e, sl=sl, ti=ti: e.activation(lnw[ti][:, sl], fA[:, sl], ACT.Ln), reads=[f"fA{u}"], writes=[f"lnw{ti}", f"lnw{ti}{u}"])

    def stage_a2_p4d(c, bi, ti, xr, xv):
        for sl, u in HS:
            S.op("s", lambda e, sl=sl, ti=ti: e.activation(rho[:, sl], lnw[ti][:, sl], ACT.Exp, scale=-1.0), reads=[f"lnw{ti}{u}"], writes=["rho", f"rho{u}"])
        for sl, u in HS:
            S.op("v", lambda e, sl=sl: e.tensor_scalar(ktok[:, sl], rho[:, sl], -1.0, 1.0, ALU.mult, ALU.add), reads=[f"rho{u}"], writes=["ktok", f"ktok{u}"])
        for sl, u in HS:
            S.op("v", lambda e, sl=sl, bi=bi: e.tensor_tensor(k2[bi][:, sl], ktok[:, sl], khat[:, sl], ALU.mult), reads=[f"ktok{u}", f"khat{u}"], writes=[f"k2_{bi}", f"k2_{bi}{u}"])

    def _bctx(c):
        return dict(bi=c % 2, ti=c % 3, tp=(c - 1) % 3, odd=(c % 2 == 1) and nch > 1,
                    xb=f"xin{c % 3}", xt=xin[c % 3], ob=f"obuf{c % 2}", ot=obuf[c % 2], pv=1 - (c % 2))

    def stage_b_p1(c):
        bi, ti, tp, odd, xb, xt, ob, ot, pv = (_bctx(c)[k] for k in ("bi", "ti", "tp", "odd", "xb", "xt", "ob", "ot", "pv"))
        for h in range(2):
            if odd:
                S.op("t", lambda e, h=h: e.matmul(pL[:, h * 512:h * 512 + 512], ngones[:], lnw[tp][:, h * 512:h * 512 + 512], start=True, stop=False), reads=["ngones", "pLr", f"lnw{tp}" + ("l" if h == 0 else "h")], writes=["pL"])
                S.op("t", lambda e, h=h: e.matmul(pL[:, h * 512:h * 512 + 512], ngtril[:], lnw[ti][:, h * 512:h * 512 + 512], start=False, stop=True), reads=["ngtril", f"lnw{ti}" + ("l" if h == 0 else "h")], writes=["pL"])
            else:
                S.op("t", lambda e, h=h: e.matmul(pL[:, h * 512:h * 512 + 512], ngtril[:], lnw[ti][:, h * 512:h * 512 + 512], start=True, stop=True), reads=["ngtril", "pLr", f"lnw{ti}" + ("l" if h == 0 else "h")], writes=["pL"])
        S.op("s", lambda e: e.activation(expl[:], pL[:], ACT.Exp), reads=["pL"], writes=["expl"])
        S.op("s", lambda e: e.activation(expnl[:], pL[:], ACT.Exp, scale=-1.0), reads=["pL"], writes=["expnl"])
        for h in range(2):
            u = "l" if h == 0 else "h"
            S.op("v", lambda e, bi=bi, h=h: e.tensor_tensor(kd[bi][:, h * 512:h * 512 + 512], k2[bi][:, h * 512:h * 512 + 512], expnl[:, h * 512:h * 512 + 512], ALU.mult), reads=[f"k2_{bi}{u}", "expnl"], writes=[f"kd{bi}", f"kd{bi}{u}"])

        if odd:
            for h in range(2):
                S.op("t", lambda e, h=h: e.matmul(pL[:, h * 512:h * 512 + 512], ngones[:], lnw[tp][:, h * 512:h * 512 + 512], start=True, stop=False), reads=["ngones", f"lnw{tp}", "expl", "expnl"], writes=["pL"])
                S.op("t", lambda e, h=h: e.matmul(pL[:, h * 512:h * 512 + 512], ngones[:], lnw[ti][:, h * 512:h * 512 + 512], start=False, stop=True), reads=["ngones", f"lnw{ti}" + ("l" if h == 0 else "h")], writes=["pL"])
            S.op("s", lambda e: e.activation(fC[:], pL[:], ACT.Exp), reads=["pL"], writes=["fC", "fCl", "fCh"])

    def stage_b_p2(c):
        bi, ti, tp, odd, xb, xt, ob, ot, pv = (_bctx(c)[k] for k in ("bi", "ti", "tp", "odd", "xb", "xt", "ob", "ot", "pv"))
        for h in range(2):
            for kv in range(4):
                S.op("t", lambda e, h=h, kv=kv, bi=bi: e.matmul(pL[:, h * 512:h * 512 + 128], vt[ti][:, (4 * h + kv) * 128:(4 * h + kv) * 128 + 128], qtb[bi][:, (4 * h + kv) * 128:(4 * h + kv) * 128 + 128], start=(kv == 0), stop=(kv == 3)), reads=[f"vt{ti}", f"qtb{bi}", "fC"], writes=["pL"])
        for h in range(2):
            S.op("v", lambda e, h=h: e.tensor_tensor(gm[:, h * 128:h * 128 + 128], pL[:, h * 512:h * 512 + 128], tmask[:], ALU.mult), reads=["pL", "tmask"], writes=["gm"])
        if odd:
            for h in range(2):
                for kv in range(4):
                    S.op("t", lambda e, h=h, kv=kv: e.matmul(pL[:, h * 512 + 128:h * 512 + 256], vt[tp][:, (4 * h + kv) * 128:(4 * h + kv) * 128 + 128], qtb[bi][:, (4 * h + kv) * 128:(4 * h + kv) * 128 + 128], start=(kv == 0), stop=(kv == 3)), reads=[f"vt{tp}", f"qtb{bi}", "gm"], writes=["pL"])
            for h in range(2):
                S.op("v", lambda e, h=h: e.tensor_copy(gmx[:, h * 128:h * 128 + 128], pL[:, h * 512 + 128:h * 512 + 256]), reads=["pL"], writes=["gmx"])

    def stage_b_p3(c):
        bi, ti, tp, odd, xb, xt, ob, ot, pv = (_bctx(c)[k] for k in ("bi", "ti", "tp", "odd", "xb", "xt", "ob", "ot", "pv"))
        for h in range(2):
            for kv in range(4):
                S.op("t", lambda e, h=h, kv=kv, bi=bi: e.matmul(pY[:, h * 512:h * 512 + 512], qtr[bi][:, (4 * h + kv) * 128:(4 * h + kv) * 128 + 128], St[h][:, kv * 512:kv * 512 + 512], start=(kv == 0), stop=False), reads=[f"qtr{bi}", f"S{h}"], writes=["pY"])
            if odd:
                S.op("t", lambda e, h=h: e.matmul(pY[:, h * 512:h * 512 + 512], gmx[:, h * 128:h * 128 + 128], kd[pv][:, h * 512:h * 512 + 512], start=False, stop=False), reads=["gmx", f"kd{pv}" + ("l" if h == 0 else "h")], writes=["pY"])
            S.op("t", lambda e, h=h, bi=bi: e.matmul(pY[:, h * 512:h * 512 + 512], gm[:, h * 128:h * 128 + 128], kd[bi][:, h * 512:h * 512 + 512], start=False, stop=True), reads=["gm", f"kd{bi}" + ("l" if h == 0 else "h")], writes=["pY"])
        S.op("v", lambda e: e.tensor_tensor(ybuf[:], expl[:], pY[:], ALU.mult), reads=["expl", "pY"], writes=["ybuf"])

        for jq in range(8):
            S.op("t", lambda e, jq=jq: e.matmul(pY[:, jq * 128:(jq + 1) * 128], ybuf[:, jq * 128:(jq + 1) * 128], ident[:], start=True, stop=True), reads=["ybuf", "ident"], writes=["pY"])
        S.op("s", lambda e: e.copy(yt[:, 0:512], pY[:, 0:512]), reads=["pY"], writes=["yt", "ytl"], cost=0.55)
        S.op("s", lambda e: e.copy(yt[:, 512:1024], pY[:, 512:1024]), reads=["pY"], writes=["yt", "yth"], cost=0.55)

    def stage_b_p4(c):
        bi, ti, tp, odd, xb, xt, ob, ot, pv = (_bctx(c)[k] for k in ("bi", "ti", "tp", "odd", "xb", "xt", "ob", "ot", "pv"))
        if odd:
            for h in range(2):
                for kv in range(4):
                    dst, dres = (pL, "pL") if kv < 2 else (pY, "pY")
                    S.op("t", lambda e, h=h, kv=kv, dst=dst: e.matmul(dst[:, (kv % 2) * 512:(kv % 2) * 512 + 512], vtok[tp][:, h * 512 + kv * 128: h * 512 + kv * 128 + 128], kd[pv][:, h * 512:h * 512 + 512], start=True, stop=False), reads=[f"vtok{tp}q{h * 2 + kv // 2}", f"kd{pv}" + ("l" if h == 0 else "h"), "yth" if kv >= 2 else "gmx"], writes=[dres])
                    S.op("t", lambda e, h=h, kv=kv, dst=dst: e.matmul(dst[:, (kv % 2) * 512:(kv % 2) * 512 + 512], vtok[ti][:, h * 512 + kv * 128: h * 512 + kv * 128 + 128], kd[bi][:, h * 512:h * 512 + 512], start=False, stop=True), reads=[f"vtok{ti}q{h * 2 + kv // 2}", f"kd{bi}" + ("l" if h == 0 else "h")], writes=[dres])
                S.op("v", lambda e, h=h: e.tensor_tensor(St[h][:, 0:1024], St[h][:, 0:1024], pL[:], ALU.add), reads=[f"S{h}", "pL"], writes=[f"S{h}"])
                S.op("v", lambda e, h=h: e.tensor_tensor(St[h][:, 1024:2048], St[h][:, 1024:2048], pY[:], ALU.add), reads=[f"S{h}", "pY"], writes=[f"S{h}"])
                for kv in range(4):
                    S.op("v", lambda e, h=h, kv=kv: e.tensor_tensor(St[h][:, kv * 512:kv * 512 + 512], St[h][:, kv * 512:kv * 512 + 512], fC[:, h * 512:h * 512 + 512], ALU.mult), reads=[f"S{h}", "fC"], writes=[f"S{h}"])

        for nh in range(2):
            for jq in range(8):
                S.op("t", lambda e, nh=nh, jq=jq: e.matmul(pL[:, nh * 512:nh * 512 + 512], yt[:, jq * 128:jq * 128 + 128], wot[:, jq * 1024 + nh * 512: jq * 1024 + nh * 512 + 512], start=(jq == 0), stop=(jq == 7)), reads=["yt" + ("l" if jq < 4 else "h"), "wot", "pLr"], writes=["pL", f"pLn{nh}"])
        S.op("v", lambda e: e.tensor_tensor(x2[:, 0:512], xt[:, 0:512], pL[:, 0:512], ALU.add), reads=[xb, "pLn0"], writes=["x2", "x2l"])
        S.op("v", lambda e: e.tensor_tensor(x2[:, 512:1024], xt[:, 512:1024], pL[:, 512:1024], ALU.add), reads=[xb, "pLn1"], writes=["x2", "x2h"])
    def stage_b_p5a(c):
        bi, ti, tp, odd, xb, xt, ob, ot, pv = (_bctx(c)[k] for k in ("bi", "ti", "tp", "odd", "xb", "xt", "ob", "ot", "pv"))
        S.op("v", lambda e: e.reduce_sum(smallA[:, 10:11], x2[:, 0:512], AX), reads=["x2l"], writes=["bA"])
        S.op("s", lambda e: e.activation(fB[:, 0:512], x2[:, 0:512], ACT.Square, accum_out=smallA[:, 11:12]), reads=["x2l"], writes=["fB", "fBl", "bB"])
        S.op("v", lambda e: e.reduce_sum(smallA[:, 12:13], x2[:, 512:1024], AX), reads=["x2h"], writes=["bC"])
        S.op("s", lambda e: e.activation(fB[:, 512:1024], x2[:, 512:1024], ACT.Square, accum_out=smallA[:, 13:14]), reads=["x2h"], writes=["fB", "fBh", "bD"])
        S.op("v", lambda e: e.tensor_tensor(smallA[:, 14:15], smallA[:, 10:11], smallA[:, 12:13], ALU.add), reads=["bA", "bC"], writes=["bs0"])
        S.op("v", lambda e: e.tensor_tensor(smallA[:, 15:16], smallA[:, 11:12], smallA[:, 13:14], ALU.add), reads=["bB", "bD"], writes=["bs1"])
        S.op("v", lambda e: e.tensor_scalar(smallA[:, 16:17], smallA[:, 14:15], INV_D, None, ALU.mult), reads=["bs0"], writes=["b0"])
        S.op("v", lambda e: e.tensor_scalar(smallA[:, 17:18], smallA[:, 16:17], smallA[:, 16:17], -EPS, ALU.mult, ALU.add), reads=["b0"], writes=["b1"])
        S.op("v", lambda e: e.scalar_tensor_tensor(smallA[:, 18:19], smallA[:, 15:16], INV_D, smallA[:, 17:18], ALU.mult, ALU.subtract), reads=["bs1", "b1"], writes=["b2"])
        S.op("s", lambda e: e.activation(smallA[:, 19:20], smallA[:, 18:19], ACT.Ln), reads=["b2"], writes=["b3"])
        S.op("s", lambda e: e.activation(smallA[:, 20:21], smallA[:, 19:20], ACT.Exp, scale=-0.5), reads=["b3"], writes=["b4"])
        S.op("v", lambda e: e.scalar_tensor_tensor(smallA[:, 21:22], smallA[:, 16:17], -1.0, smallA[:, 20:21], ALU.mult, ALU.mult), reads=["b0", "b4"], writes=["b5"])
        S.op("s", lambda e: e.activation(x2ln[:, 0:512], x2[:, 0:512], ACT.Identity, bias=smallA[:, 21:22], scale=smallA[:, 20:21]), reads=["x2l", "b4", "b5"], writes=["x2ln", "x2lnl"])
        S.op("s", lambda e: e.activation(x2ln[:, 512:1024], x2[:, 512:1024], ACT.Identity, bias=smallA[:, 21:22], scale=smallA[:, 20:21]), reads=["x2h", "b4", "b5"], writes=["x2ln", "x2lnh"])

    def stage_b_p5b(c):
        bi, ti, tp, odd, xb, xt, ob, ot, pv = (_bctx(c)[k] for k in ("bi", "ti", "tp", "odd", "xb", "xt", "ob", "ot", "pv"))
        for jq in range(8):
            S.op("t", lambda e, jq=jq: e.matmul(pL[:, jq * 128:(jq + 1) * 128], x2ln[:, jq * 128:(jq + 1) * 128], ident[:], start=True, stop=True), reads=["x2lnl" if jq < 4 else "x2lnh", "ident"], writes=["pL", "pLhf0" if jq < 4 else "pLhf1"])
        for u in range(2):
            S.op("v", lambda e, u=u: e.tensor_scalar(xqt[:, u * 512:u * 512 + 512], pL[:, u * 512:u * 512 + 512], scal[:, 2:3], None, ALU.mult), reads=[f"pLhf{u}", "scal2"], writes=["xqt", f"xqth{u}", "pLr"])
        for nh in range(2):
            for j in range(8):
                S.op("t", lambda e, nh=nh, j=j: e.matmul(pY[:, nh * 512:nh * 512 + 512], xqt[:, j * 128:j * 128 + 128], fkq[:, j * 1024 + nh * 512: j * 1024 + nh * 512 + 512], start=(j == 0), stop=(j == 7)), reads=["xqth0" if j < 4 else "xqth1", "fkq"], writes=["pY"])
    def stage_b_p5c(c):
        bi, ti, tp, odd, xb, xt, ob, ot, pv = (_bctx(c)[k] for k in ("bi", "ti", "tp", "odd", "xb", "xt", "ob", "ot", "pv"))
        S.op("v", lambda e: e.tensor_scalar(ybuf[:], pY[:], scal[:, 4:5], None, ALU.mult), reads=["pY", "scal4"], writes=["ybuf"])
        S.op("v", lambda e: e.tensor_tensor(x2ln[:], ybuf[:], biasr[:], ALU.add), reads=["ybuf", "biasr"], writes=["x2ln"])
        S.op("v", lambda e: e.tensor_tensor(ot[:], x2[:], x2ln[:], ALU.add), reads=["x2", "x2l", "x2ln"], writes=[ob])
        S.op("g", lambda e, c=c, ot=ot: e.dma_start(d_out[c * 128:(c + 1) * 128, :], ot[:]), reads=[ob], dma=True)

        if dbg and c == 0:
            for nm, src, res in [("dxln", xln[0], "xln0"), ("dktok", ktok, "ktok"), ("dkhat", khat, "khat"),
                                 ("dlnw", lnw[0], "lnw0"), ("dexpl", expl, "expl"), ("dkd", kd[0], "kd0"),
                                 ("dgm", gm, "gm"), ("dy", ybuf, "ybuf"), ("dx2", x2, "x2"),
                                 ("dxq", xqt, "xqt"), ("dqt", qtr[0], "qtr0")]:
                ap = src[:].bitcast(F32) if src in (lnw[0], qtr[0]) else src[:]
                S.op("sy", lambda e, nm=nm, ap=ap: e.dma_start(dbg_outs[nm], ap), reads=[res], dma=True)
        if dbg and c == 1:
            S.op("sy", lambda e: e.dma_start(dbg_outs["dstate0"], St[0][:].bitcast(F32)), reads=["S0"], dma=True)

    stage_a1(0)
    if nch > 1:
        stage_a1(1)
    stage_a2(0)
    for c in range(nch):
        stage_b_p1(c)
        if c + 1 < nch:
            stage_a2(c + 1, parts=(1,))
        stage_b_p2(c)
        if c + 1 < nch:
            stage_a2(c + 1, parts=(2, 3))
        stage_b_p3(c)
        if c + 1 < nch:
            stage_a2(c + 1, parts=(4,))
        stage_b_p4(c)
        if c + 1 < nch:
            stage_a2(c + 1, parts=(5,))
        stage_b_p5a(c)
        if c + 1 < nch:
            stage_a2(c + 1, parts=(6,))
        stage_b_p5b(c)
        if c + 1 < nch:
            stage_a2(c + 1, parts=(7,))
        if c + 2 < nch:
            stage_a1(c + 2)
        stage_b_p5c(c)

    S.finalize()
    with nc.Block() as block:
        sems = {}
        ctx = []
        for en in ENGINES:
            sem = nc.semaphore(f"sem_{en}")
            ctx.append(sem)
            sems[en] = sem.__enter__()

        @block.sync
        def _(e):
            S.emit_engine("sy", e, sems)

        @block.gpsimd
        def _(e):
            S.emit_engine("g", e, sems)

        @block.tensor
        def _(e):
            S.emit_engine("t", e, sems)

        @block.vector
        def _(e):
            S.emit_engine("v", e, sems)

        @block.scalar
        def _(e):
            S.emit_engine("s", e, sems)

    for t in reversed(sb):
        t.__exit__(None, None, None)
    for sem in reversed(ctx):
        sem.__exit__(None, None, None)
    return nc


def _pack_w(m):
    return np.ascontiguousarray(m.reshape(8, 128, 1024).transpose(1, 0, 2).reshape(128, 8192))


def make_inputs(inputs, T=4096):
    f32 = np.float32
    bf = ml_dtypes.bfloat16
    Wq = np.asarray(inputs["L0_attn_Wq"], dtype=f32)
    Wk = np.asarray(inputs["L0_attn_Wk"], dtype=f32)
    Wv = np.asarray(inputs["L0_attn_Wv"], dtype=f32)
    Wo = np.asarray(inputs["L0_attn_Wo"], dtype=f32)
    fker = np.asarray(inputs["L0_ffn_kernel"])
    if fker.dtype != bf:
        fker = fker.view(bf) if fker.dtype == np.uint16 else fker.astype(bf)
    bias = np.asarray(inputs["L0_ffn_bias"])
    if bias.dtype != bf:
        bias = bias.view(bf) if bias.dtype == np.uint16 else bias.astype(bf)
    wg = np.asarray(inputs["L0_attn_W_gamma"], dtype=f32).reshape(D)
    bg = np.asarray(inputs["L0_attn_b_gamma"], dtype=f32).reshape(D)
    wl = np.asarray(inputs["L0_attn_W_lambda"], dtype=f32).reshape(D)
    bl = np.asarray(inputs["L0_attn_b_lambda"], dtype=f32).reshape(D)
    common = {
        "wqt": _pack_w(np.ascontiguousarray(Wq.T)),
        "wkt": _pack_w(np.ascontiguousarray(Wk.T)),
        "wvt": _pack_w(np.ascontiguousarray(Wv.T)),
        "wot": _pack_w(np.ascontiguousarray(Wo.T)),
        "fk": _pack_w(np.ascontiguousarray(fker)),
        "biasrep": np.ascontiguousarray(np.broadcast_to(bias.reshape(1, D), (128, D))),
        "wgrep": np.ascontiguousarray(np.broadcast_to(wg.reshape(1, D), (128, D))),
        "bgrep": np.ascontiguousarray(np.broadcast_to(bg.reshape(1, D), (128, D))),
        "wlrep": np.ascontiguousarray(np.broadcast_to(wl.reshape(1, D), (128, D))),
        "blrep": np.ascontiguousarray(np.broadcast_to(bl.reshape(1, D), (128, D))),
        "sin": np.asarray(inputs["L0_ffn_input_scale"], dtype=f32).reshape(1, 1),
        "sk": np.asarray(inputs["L0_ffn_kernel_scale"], dtype=f32).reshape(1, 1),
        "negtril": np.ascontiguousarray(-np.triu(np.ones((128, 128), dtype=f32))),
        "negones": np.ascontiguousarray(-np.ones((128, 128), dtype=f32)),
        "trilmask": np.ascontiguousarray(np.triu(np.ones((128, 128))).astype(bf)),
        "ident": np.ascontiguousarray(np.eye(128).astype(bf)),
    }
    return common


def kernel(**inputs):
    x = np.asarray(inputs["L0_attn_x"], dtype=np.float32)
    B, T, Dd = x.shape
    common = make_inputs(inputs, T=T)
    nc = build(T=T)
    in_maps = [dict(common, x=np.ascontiguousarray(x[b])) for b in range(B)]
    res = run_bass_kernel_spmd(nc, in_maps, core_ids=list(range(B)))
    out = np.stack([np.asarray(res.results[b]["out"]) for b in range(B)], axis=0)
    return out


# version 28 (final)
# speedup vs baseline: 1.0197x; 1.0197x over previous
import numpy as np
import ml_dtypes
import concourse.bass as bass
import concourse.mybir as mybir
from concourse.bass_utils import run_bass_kernel_spmd

F32 = mybir.dt.float32
F32R = mybir.dt.float32r
BF16 = mybir.dt.bfloat16
F8 = mybir.dt.float8e4
AX = mybir.AxisListType.X
ALU = mybir.AluOpType
ACT = mybir.ActivationFunctionType

D = 1024
H = 2
DK = 512
C = 128
LNW_CLAMP = 0.55
EPS = 1e-5
INV_D = 1.0 / 1024.0

ENGINES = ("sy", "g", "t", "v", "s")


class Sched:

    FULLW = ("expl", "expnl", "ybuf", "ktok", "xlnT", "yt", "xln0", "xln1",
             "x2ln", "vtok0", "vtok1", "vtok2", "vt0", "vt1", "vt2",
             "qtb0", "qtb1", "qtr0", "qtr1", "fC", "fB")

    def __init__(self, reorder=True):
        self.do_reorder = reorder
        self.rawops = {e: [] for e in ENGINES}
        self.ops = {e: [] for e in ENGINES}
        self.n = {e: 0 for e in ENGINES}
        self.w = {}
        self.r = {}
        self.gseq = []

    def op(self, eng, fn, reads=(), writes=(), dma=False, cost=None):
        deps = set()
        odeps = set()
        if dma and self.n[eng]:
            deps.add((eng, self.n[eng] - 1))

        def need(de, di, d=deps):
            d.add((de, di))

        for res in reads:
            lw = self.w.get(res)
            if lw:
                need(*lw)
        for res in writes:
            lw = self.w.get(res)
            if lw:
                de, di = lw
                if not (de == eng == "t" and res.startswith("p")):
                    need(de, di)
                else:
                    need(de, di, odeps)
        for res in writes:
            for rr in self.r.get(res, ()):
                need(*rr)
        me = (eng, self.n[eng])
        self.n[eng] += 1
        for res in reads:
            self.r.setdefault(res, []).append(me)
        for res in writes:
            self.w[res] = me
            self.r[res] = []
        if cost is None:
            if dma:
                cost = 1.4
            elif eng == "t":
                cost = 0.10 if "ident" in reads else 0.24
            elif writes and all(w[:2] in ("sm", "mv", "ss", "sc") for w in writes):
                cost = 0.12
            elif any(w in Sched.FULLW for w in writes):
                cost = 1.05
            else:
                cost = 0.6
        self.gseq.append(me)
        self.ops[eng].append((deps, fn, dma, odeps, cost))

    def reorder(self):
        import heapq
        preds = {}
        succs = {k: [] for k in self.gseq}
        cost = {}
        seqno = {k: i for i, k in enumerate(self.gseq)}
        allow = getattr(Sched, "ALLOW", set(ENGINES))
        for e in ENGINES:
            for i, (deps, fn, dma, odeps, c) in enumerate(self.ops[e]):
                k = (e, i)
                ps = deps | odeps
                if e not in allow and i > 0:
                    ps = ps | {(e, i - 1)}
                preds[k] = list(ps)
                cost[k] = c
                for p in ps:
                    succs[p].append(k)
        rank = {}
        for k in reversed(self.gseq):
            rank[k] = cost[k] + max((rank[s] for s in succs[k]), default=0.0)
        if self.do_reorder:
            npred = {k: len(preds[k]) for k in self.gseq}
            finish = {}
            clock = {e: 0.0 for e in ENGINES}
            ready = {e: [] for e in ENGINES}
            for k in self.gseq:
                if npred[k] == 0:
                    heapq.heappush(ready[k[0]], (-rank[k], seqno[k], k))
            order = {e: [] for e in ENGINES}
            nleft = len(self.gseq)
            while nleft:
                e = min((x for x in ENGINES if ready[x]), key=lambda x: clock[x])
                popped = []
                for _ in range(min(24, len(ready[e]))):
                    popped.append(heapq.heappop(ready[e]))
                best, bkey = None, None
                for it in popped:
                    k = it[2]
                    rt = max((finish[p] for p in preds[k]), default=0.0)
                    key = (0 if rt <= clock[e] + 1e-9 else 1, rt if rt > clock[e] else 0.0, it[0], it[1])
                    if best is None or key < best:
                        best, bkey = key, it
                for it in popped:
                    if it is not bkey:
                        heapq.heappush(ready[e], it)
                k = bkey[2]
                rt = max((finish[p] for p in preds[k]), default=0.0)
                st = max(clock[e], rt)
                finish[k] = st + cost[k]
                clock[e] = finish[k]
                order[e].append(k)
                nleft -= 1
                for sk in succs[k]:
                    npred[sk] -= 1
                    if npred[sk] == 0:
                        heapq.heappush(ready[sk[0]], (-rank[sk], seqno[sk], sk))
            self.est_makespan = max(clock.values())
        else:
            order = {e: [(e, i) for i in range(len(self.ops[e]))] for e in ENGINES}
        newpos = {e: {} for e in ENGINES}
        for e in ENGINES:
            for ni, (_, oi) in enumerate(order[e]):
                newpos[e][oi] = ni
        newops = {e: [] for e in ENGINES}
        for e in ENGINES:
            for _, oi in order[e]:
                deps, fn, dma, odeps, c = self.ops[e][oi]
                nd = {}
                for de, di in deps:
                    ni = newpos[de][di]
                    if nd.get(de, -1) < ni:
                        nd[de] = ni
                for de, ni in list(nd.items()):
                    assert (de != e) or ni < newpos[e][oi]
                newops[e].append((nd, fn, dma))
        self.ops = newops

    def raw(self, eng, fn):
        self.rawops[eng].append(fn)

    def finalize(self):
        self.reorder()
        referenced = {e: set() for e in ENGINES}
        for e in ENGINES:
            for deps, _, _ in self.ops[e]:
                for de, di in deps.items():
                    referenced[de].add(di)
        self.incnum = {e: [] for e in ENGINES}
        self.incs = {e: [] for e in ENGINES}
        self.total = {}
        for e in ENGINES:
            run = 0
            for i, (_, _, is_dma) in enumerate(self.ops[e]):
                amt = 16 if is_dma else (1 if i in referenced[e] else 0)
                run += amt
                self.incs[e].append(amt)
                self.incnum[e].append(run)
            self.total[e] = run

    def emit_engine(self, ename, e, sems):
        for fn in self.rawops[ename]:
            fn(e)
        lastw = {}
        for i, (deps, fn, is_dma) in enumerate(self.ops[ename]):
            for de in ENGINES:
                di = deps.get(de)
                if di is None:
                    continue
                dv = self.incnum[de][di]
                assert self.incs[de][di] > 0, (ename, i, de, di)
                if lastw.get(de, 0) >= dv:
                    continue
                e.wait_ge(sems[de], dv)
                lastw[de] = dv
            inst = fn(e)
            amt = self.incs[ename][i]
            if amt:
                inst.then_inc(sems[ename], amt)
        if ename in ("sy", "g") and self.total[ename]:
            e.wait_ge(sems[ename], self.total[ename])


def build(T=4096, dbg=False):
    nch = T // C
    assert nch % 2 == 0 or nch == 1
    nc = bass.Bass("TRN2", target_bir_lowering=False)

    dx = nc.dram_tensor("x", [T, D], F32, kind="ExternalInput").ap()
    d_wqt = nc.dram_tensor("wqt", [128, 8192], F32, kind="ExternalInput").ap()
    d_wkt = nc.dram_tensor("wkt", [128, 8192], F32, kind="ExternalInput").ap()
    d_wvt = nc.dram_tensor("wvt", [128, 8192], F32, kind="ExternalInput").ap()
    d_wot = nc.dram_tensor("wot", [128, 8192], F32, kind="ExternalInput").ap()
    d_fk = nc.dram_tensor("fk", [128, 8192], BF16, kind="ExternalInput").ap()
    d_bias = nc.dram_tensor("biasrep", [128, D], BF16, kind="ExternalInput").ap()
    d_wg = nc.dram_tensor("wgrep", [128, D], F32, kind="ExternalInput").ap()
    d_bg = nc.dram_tensor("bgrep", [128, D], F32, kind="ExternalInput").ap()
    d_wl = nc.dram_tensor("wlrep", [128, D], F32, kind="ExternalInput").ap()
    d_bl = nc.dram_tensor("blrep", [128, D], F32, kind="ExternalInput").ap()
    d_sin = nc.dram_tensor("sin", [1, 1], F32, kind="ExternalInput").ap()
    d_sk = nc.dram_tensor("sk", [1, 1], F32, kind="ExternalInput").ap()
    d_ngtril = nc.dram_tensor("negtril", [128, 128], F32R, kind="ExternalInput").ap()
    d_ngones = nc.dram_tensor("negones", [128, 128], F32R, kind="ExternalInput").ap()
    d_tmask = nc.dram_tensor("trilmask", [128, 128], BF16, kind="ExternalInput").ap()
    d_ident = nc.dram_tensor("ident", [128, 128], BF16, kind="ExternalInput").ap()
    d_out = nc.dram_tensor("out", [T, D], F32, kind="ExternalOutput").ap()

    dbg_outs = {}
    if dbg:
        for name, shape, dt in [
            ("dxln", [128, D], BF16), ("dktok", [128, D], BF16),
            ("dkhat", [128, D], BF16), ("dlnw", [128, D], F32),
            ("dexpl", [128, D], BF16), ("dkd", [128, D], BF16),
            ("dgm", [128, 256], BF16), ("dy", [128, D], BF16),
            ("dx2", [128, D], F32), ("dstate0", [128, 2048], F32),
            ("dxq", [128, D], F8), ("dqt", [128, D], F32),
        ]:
            dbg_outs[name] = nc.dram_tensor(name, shape, dt, kind="ExternalOutput").ap()

    sb = []

    def sbuf(name, shape, dt):
        t = nc.sbuf_tensor(name, shape, dt)
        sb.append(t)
        return t.__enter__()

    def psum(name, shape):
        t = nc.psum_tensor(name, shape, F32)
        sb.append(t)
        return t.__enter__()

    wqt = sbuf("wqt_s", [128, 8192], BF16)
    wkt = sbuf("wkt_s", [128, 8192], BF16)
    wvt = sbuf("wvt_s", [128, 8192], BF16)
    wot = sbuf("wot_s", [128, 8192], BF16)
    fkq = sbuf("fkq_s", [128, 8192], F8)
    biasr = sbuf("biasr_s", [128, D], BF16)
    wgr = sbuf("wgr_s", [128, D], BF16)
    bgr = sbuf("bgr_s", [128, D], F32)
    wlr = sbuf("wlr_s", [128, D], BF16)
    blr = sbuf("blr_s", [128, D], F32)
    ngtril = sbuf("ngtril_s", [128, 128], F32R)
    ngones = sbuf("ngones_s", [128, 128], F32R)
    tmask = sbuf("tmask_s", [128, 128], BF16)
    ident = sbuf("ident_s", [128, 128], BF16)
    scal = sbuf("scal_s", [128, 8], F32)
    St = [sbuf("state0", [128, 2048], F32R), sbuf("state1", [128, 2048], F32R)]
    xin = [sbuf(f"xin{i}", [128, D], F32) for i in range(3)]
    obuf = [sbuf(f"obuf{i}", [128, D], F32) for i in range(2)]
    qtr = [sbuf(f"qtr{i}", [128, D], F32R) for i in range(2)]
    qtb = [sbuf(f"qtb{i}", [128, D], BF16) for i in range(2)]
    vtok = [sbuf(f"vtok{i}", [128, D], BF16) for i in range(3)]
    vt = [sbuf(f"vt{i}", [128, D], BF16) for i in range(3)]
    lnw = [sbuf(f"lnw{i}", [128, D], F32R) for i in range(3)]
    k2 = [sbuf(f"k2_{i}", [128, D], BF16) for i in range(2)]
    kd = [sbuf(f"kd{i}", [128, D], BF16) for i in range(2)]
    xln = [sbuf(f"xln{i}", [128, D], BF16) for i in range(2)]
    xlnT = sbuf("xlnT", [128, D], BF16)
    ktok = sbuf("ktok", [128, D], BF16)
    khat = sbuf("khat", [128, D], BF16)
    fA = sbuf("fA", [128, D], F32)
    fB = sbuf("fB", [128, D], F32)
    fC = sbuf("fC", [128, D], F32)
    rho = sbuf("rho", [128, D], BF16)
    expl = sbuf("expl", [128, D], BF16)
    expnl = sbuf("expnl", [128, D], BF16)
    gm = sbuf("gm", [128, 256], BF16)
    gmx = sbuf("gmx", [128, 256], BF16)
    ybuf = sbuf("ybuf", [128, D], BF16)
    yt = sbuf("yt", [128, D], BF16)
    x2 = sbuf("x2buf", [128, D], F32)
    x2ln = sbuf("x2ln", [128, D], BF16)
    xqt = sbuf("xqt", [128, D], F8)
    mv = sbuf("mv", [128, 2], F32)
    smallA = sbuf("smallA", [128, 24], F32)
    ssb = sbuf("ssb", [128, 2], F32)
    rk = sbuf("rk", [128, 2], F32)
    pA = psum("pA", [128, 1024])
    pB = psum("pB", [128, 1024])
    pY = psum("pY", [128, 1024])
    pL = psum("pL", [128, 1024])

    S = Sched()

    def _load_lnexp_table(e):
        return e.add_instruction(mybir.InstLoadActFuncSet(
            name=e.bass.get_next_instruction_name(), act_func_set_id=6, ins=[], outs=[]))
    S.raw("s", _load_lnexp_table)
    S.op("sy", lambda e: e.dma_start(xin[0][:], dx[0:128, :]), writes=["xin0"], dma=True)
    if nch > 1:
        S.op("sy", lambda e: e.dma_start(xin[1][:], dx[128:256, :]), writes=["xin1"], dma=True)
    S.op("sy", lambda e: e.dma_start(ngtril[:], d_ngtril), writes=["ngtril"], dma=True)
    S.op("sy", lambda e: e.dma_start(ngones[:], d_ngones), writes=["ngones"], dma=True)
    S.op("sy", lambda e: e.dma_start(tmask[:], d_tmask), writes=["tmask"], dma=True)
    S.op("sy", lambda e: e.dma_start(ident[:], d_ident), writes=["ident"], dma=True)
    S.op("sy", lambda e: e.dma_start(scal[:, 0:1], bass.AP(d_sin.tensor, 0, [[0, 128], [1, 1]])), writes=["scal01"], dma=True)
    S.op("sy", lambda e: e.dma_start(scal[:, 1:2], bass.AP(d_sk.tensor, 0, [[0, 128], [1, 1]])), writes=["scal01"], dma=True)
    S.op("sy", lambda e: e.dma_start(bgr[:], d_bg), writes=["bgr"], dma=True)
    S.op("sy", lambda e: e.dma_start(blr[:], d_bl), writes=["blr"], dma=True)
    S.op("sy", lambda e: e.dma_start(biasr[:], d_bias), writes=["biasr"], dma=True)
    S.op("sy", lambda e: e.dma_start(wqt[:], d_fk), writes=["wqt"], dma=True)
    S.op("v", lambda e: e.reciprocal(scal[:, 2:3], scal[:, 0:1]), reads=["scal01"], writes=["scal2"])
    S.op("v", lambda e: e.reciprocal(scal[:, 3:4], scal[:, 1:2]), reads=["scal01"], writes=["scal3"])
    S.op("v", lambda e: e.tensor_tensor(scal[:, 4:5], scal[:, 0:1], scal[:, 1:2], ALU.mult), reads=["scal01"], writes=["scal4"])
    S.op("v", lambda e: e.tensor_scalar(fkq[:], wqt[:], scal[:, 3:4], None, ALU.mult), reads=["wqt", "scal3"], writes=["fkq"])
    S.op("g", lambda e: e.dma_start(wgr[:], d_wg), writes=["wgr"], dma=True)
    S.op("g", lambda e: e.dma_start(wlr[:], d_wl), writes=["wlr"], dma=True)
    S.op("g", lambda e: e.dma_start(wkt[:], d_wkt), writes=["wkt"], dma=True)
    S.op("g", lambda e: e.dma_start(wqt[:], d_wqt), writes=["wqt"], dma=True)
    S.op("g", lambda e: e.dma_start(wvt[:], d_wvt), writes=["wvt"], dma=True)
    S.op("g", lambda e: e.dma_start(wot[:], d_wot), writes=["wot"], dma=True)
    S.op("v", lambda e: e.memset(St[0][:].bitcast(F32), 0.0), writes=["S0"])
    S.op("v", lambda e: e.memset(St[1][:].bitcast(F32), 0.0), writes=["S1"])

    def stage_a1(c):
        bi = c % 2
        xb = f"xin{c % 3}"
        xt = xin[c % 3]
        if c >= 2:
            S.op("sy", lambda e, c=c, xt=xt: e.dma_start(xt[:], dx[c * 128:(c + 1) * 128, :]), writes=[xb], dma=True)
        S.op("v", lambda e: e.reduce_sum(mv[:, 0:1], xt[:], AX), reads=[xb], writes=["mv"])
        S.op("s", lambda e: e.activation(fB[:], xt[:], ACT.Square, accum_out=mv[:, 1:2]), reads=[xb], writes=["fB", "fBl", "fBh", "mv"])
        S.op("v", lambda e: e.tensor_scalar(smallA[:, 0:1], mv[:, 0:1], INV_D, None, ALU.mult), reads=["mv"], writes=["sm0"])
        S.op("v", lambda e: e.tensor_scalar(smallA[:, 1:2], smallA[:, 0:1], smallA[:, 0:1], -EPS, ALU.mult, ALU.add), reads=["sm0"], writes=["sm1"])
        S.op("v", lambda e: e.scalar_tensor_tensor(smallA[:, 2:3], mv[:, 1:2], INV_D, smallA[:, 1:2], ALU.mult, ALU.subtract), reads=["mv", "sm1"], writes=["sm2"])
        S.op("s", lambda e: e.activation(smallA[:, 3:4], smallA[:, 2:3], ACT.Ln), reads=["sm2"], writes=["sm3"])
        S.op("s", lambda e: e.activation(smallA[:, 4:5], smallA[:, 3:4], ACT.Exp, scale=-0.5), reads=["sm3"], writes=["sm4"])
        S.op("v", lambda e: e.scalar_tensor_tensor(smallA[:, 5:6], smallA[:, 0:1], -1.0, smallA[:, 4:5], ALU.mult, ALU.mult), reads=["sm0", "sm4"], writes=["sm5"])
        S.op("s", lambda e, bi=bi: e.activation(xln[bi][:, 0:512], xt[:, 0:512], ACT.Identity, bias=smallA[:, 5:6], scale=smallA[:, 4:5]), reads=[xb, "sm4", "sm5"], writes=[f"xln{bi}", f"xln{bi}l"], cost=0.55)
        S.op("s", lambda e, bi=bi: e.activation(xln[bi][:, 512:1024], xt[:, 512:1024], ACT.Identity, bias=smallA[:, 5:6], scale=smallA[:, 4:5]), reads=[xb, "sm4", "sm5"], writes=[f"xln{bi}", f"xln{bi}h"], cost=0.55)

    def stage_a2(c, parts=(1, 2, 3, 4, 5, 6, 7)):
        bi = c % 2
        ti = c % 3
        xr = f"xln{bi}"
        xv = xln[bi]
        fns = {1: stage_a2_p1, 2: stage_a2_p2, 3: stage_a2_p3, 4: stage_a2_p4a,
               5: stage_a2_p4b, 6: stage_a2_p4c, 7: stage_a2_p4d}
        for p in parts:
            fns[p](c, bi, ti, xr, xv)

    def stage_a2_p1(c, bi, ti, xr, xv):
        for jq in range(8):
            S.op("t", lambda e, jq=jq, xv=xv: e.matmul(pB[:, jq * 128:(jq + 1) * 128], xv[:, jq * 128:(jq + 1) * 128], ident[:], start=True, stop=True), reads=[xr + ("l" if jq < 4 else "h"), "ident"], writes=["pB"])
        S.op("s", lambda e: e.copy(xlnT[:, 0:512], pB[:, 0:512]), reads=["pB"], writes=["xlnT", "xlnTl"], cost=0.55)
        S.op("s", lambda e: e.copy(xlnT[:, 512:1024], pB[:, 512:1024]), reads=["pB"], writes=["xlnT", "xlnTh"], cost=0.55)

        for nh in range(2):
            for j in range(8):
                S.op("t", lambda e, nh=nh, j=j: e.matmul(pA[:, nh * 512:nh * 512 + 512], xlnT[:, j * 128:j * 128 + 128], wkt[:, j * 1024 + nh * 512: j * 1024 + nh * 512 + 512], start=(j == 0), stop=(j == 7)), reads=["xlnT" + ("l" if j < 4 else "h"), "wkt"], writes=["pA"])
        S.op("s", lambda e: e.copy(ktok[:], pA[:]), reads=["pA"], writes=["ktok", "ktokl", "ktokh"])

        for h in range(2):
            S.op("s", lambda e, h=h: e.activation(fA[:, h * 512:h * 512 + 512], ktok[:, h * 512:h * 512 + 512], ACT.Square, accum_out=ssb[:, h:h + 1]), reads=["ktok"], writes=["fA", "fAl", "fAh", "ssb"])
        S.op("v", lambda e: e.tensor_scalar(ssb[:], ssb[:], 1e-24, None, ALU.max), reads=["ssb"], writes=["ssb"])
        S.op("s", lambda e: e.activation(smallA[:, 6:8], ssb[:], ACT.Ln), reads=["ssb"], writes=["sm67"])
        S.op("s", lambda e: e.activation(rk[:], smallA[:, 6:8], ACT.Exp, scale=-0.5), reads=["sm67"], writes=["rk"])
        for h in range(2):
            S.op("v", lambda e, h=h: e.tensor_scalar(khat[:, h * 512:h * 512 + 512], ktok[:, h * 512:h * 512 + 512], rk[:, h:h + 1], None, ALU.mult), reads=["ktok", "rk"], writes=["khat", "khat" + ("l" if h == 0 else "h")])

    def stage_a2_p2(c, bi, ti, xr, xv):
        for nh in range(2):
            for j in range(8):
                S.op("t", lambda e, nh=nh, j=j: e.matmul(pA[:, nh * 512:nh * 512 + 512], xlnT[:, j * 128:j * 128 + 128], wqt[:, j * 1024 + nh * 512: j * 1024 + nh * 512 + 512], start=(j == 0), stop=(j == 7)), reads=["xlnT" + ("l" if j < 4 else "h"), "wqt"], writes=["pA"])
        S.op("s", lambda e, xv=xv: e.copy(xv[:, 0:512], pA[:, 0:512]), reads=["pA"], writes=[xr, xr + "l"], cost=0.55)
        S.op("s", lambda e, xv=xv: e.copy(xv[:, 512:1024], pA[:, 512:1024]), reads=["pA"], writes=[xr, xr + "h"], cost=0.55)
        for jq in range(8):
            S.op("t", lambda e, jq=jq, xv=xv: e.matmul(pB[:, jq * 128:(jq + 1) * 128], xv[:, jq * 128:(jq + 1) * 128], ident[:], start=True, stop=True), reads=[xr + ("l" if jq < 4 else "h"), "ident"], writes=["pB"])
        S.op("s", lambda e, bi=bi: e.copy(qtr[bi][:], pB[:]), reads=["pB"], writes=[f"qtr{bi}"])
        S.op("s", lambda e, bi=bi: e.copy(qtb[bi][:], pB[:]), reads=["pB"], writes=[f"qtb{bi}"])

    def stage_a2_p3(c, bi, ti, xr, xv):
        for nh in range(2):
            for j in range(8):
                S.op("t", lambda e, nh=nh, j=j: e.matmul(pA[:, nh * 512:nh * 512 + 512], xlnT[:, j * 128:j * 128 + 128], wvt[:, j * 1024 + nh * 512: j * 1024 + nh * 512 + 512], start=(j == 0), stop=(j == 7)), reads=["xlnT" + ("l" if j < 4 else "h"), "wvt"], writes=["pA"])
        for q in range(4):
            if q % 2 == 0:
                S.op("s", lambda e, ti=ti, q=q: e.copy(vtok[ti][:, q * 256:q * 256 + 256], pA[:, q * 256:q * 256 + 256]), reads=["pA"], writes=[f"vtok{ti}", f"vtok{ti}q{q}"], cost=0.3)
            else:
                S.op("v", lambda e, ti=ti, q=q: e.tensor_copy(vtok[ti][:, q * 256:q * 256 + 256], pA[:, q * 256:q * 256 + 256]), reads=["pA"], writes=[f"vtok{ti}", f"vtok{ti}q{q}"], cost=0.3)
        for jq in range(8):
            S.op("t", lambda e, jq=jq, ti=ti: e.matmul(pB[:, jq * 128:(jq + 1) * 128], vtok[ti][:, jq * 128:(jq + 1) * 128], ident[:], start=True, stop=True), reads=[f"vtok{ti}q{jq // 2}", "ident"], writes=["pB"])
        S.op("s", lambda e, ti=ti: e.copy(vt[ti][:], pB[:]), reads=["pB"], writes=[f"vt{ti}"])

    HS = [(slice(0, 512), "l"), (slice(512, 1024), "h")]

    def stage_a2_p4a(c, bi, ti, xr, xv):
        for sl, u in HS:
            S.op("v", lambda e, sl=sl: e.tensor_tensor(rho[:, sl], khat[:, sl], wgr[:, sl], ALU.mult), reads=[f"khat{u}", "wgr"], writes=["rho", f"rho{u}"])
        for sl, u in HS:
            S.op("v", lambda e, sl=sl: e.tensor_tensor(fB[:, sl], rho[:, sl], bgr[:, sl], ALU.add), reads=[f"rho{u}", "bgr"], writes=["fB", f"fB{u}"])
        for sl, u in HS:
            S.op("s", lambda e, sl=sl: e.activation(fA[:, sl], fB[:, sl], ACT.Exp, scale=-1.0), reads=[f"fB{u}"], writes=["fA", f"fA{u}"])

    def stage_a2_p4b(c, bi, ti, xr, xv):
        for sl, u in HS:
            S.op("v", lambda e, sl=sl: e.tensor_tensor(rho[:, sl], khat[:, sl], wlr[:, sl], ALU.mult), reads=[f"khat{u}", "wlr"], writes=["rho", f"rho{u}"])
        for sl, u in HS:
            S.op("v", lambda e, sl=sl: e.tensor_tensor(fB[:, sl], rho[:, sl], blr[:, sl], ALU.add), reads=[f"rho{u}", "blr"], writes=["fB", f"fB{u}"])
        for sl, u in HS:
            S.op("s", lambda e, sl=sl: e.activation(fC[:, sl], fB[:, sl], ACT.Exp), reads=[f"fB{u}"], writes=["fC", f"fC{u}"])

    def stage_a2_p4c(c, bi, ti, xr, xv):
        for sl, u in HS:
            S.op("v", lambda e, sl=sl: e.tensor_scalar(fB[:, sl], fC[:, sl], 1.0, None, ALU.add), reads=[f"fC{u}"], writes=["fB", f"fB{u}"])
        for sl, u in HS:
            S.op("v", lambda e, sl=sl: e.scalar_tensor_tensor(fC[:, sl], fA[:, sl], 1.0, fB[:, sl], ALU.add, ALU.mult), reads=[f"fA{u}", f"fB{u}"], writes=["fC", f"fC{u}"])
        for sl, u in HS:
            S.op("v", lambda e, sl=sl: e.tensor_scalar(fA[:, sl], fC[:, sl], float(np.exp(np.float32(LNW_CLAMP))), None, ALU.min), reads=[f"fC{u}"], writes=["fA", f"fA{u}"])
        for sl, u in HS:
            S.op("s", lambda e, sl=sl, ti=ti: e.activation(lnw[ti][:, sl], fA[:, sl], ACT.Ln), reads=[f"fA{u}"], writes=[f"lnw{ti}", f"lnw{ti}{u}"])

    def stage_a2_p4d(c, bi, ti, xr, xv):
        for sl, u in HS:
            S.op("s", lambda e, sl=sl, ti=ti: e.activation(rho[:, sl], lnw[ti][:, sl], ACT.Exp, scale=-1.0), reads=[f"lnw{ti}{u}"], writes=["rho", f"rho{u}"])
        for sl, u in HS:
            S.op("v", lambda e, sl=sl: e.tensor_scalar(ktok[:, sl], rho[:, sl], -1.0, 1.0, ALU.mult, ALU.add), reads=[f"rho{u}"], writes=["ktok", f"ktok{u}"])
        for sl, u in HS:
            S.op("v", lambda e, sl=sl, bi=bi: e.tensor_tensor(k2[bi][:, sl], ktok[:, sl], khat[:, sl], ALU.mult), reads=[f"ktok{u}", f"khat{u}"], writes=[f"k2_{bi}", f"k2_{bi}{u}"])

    def _bctx(c):
        return dict(bi=c % 2, ti=c % 3, tp=(c - 1) % 3, odd=(c % 2 == 1) and nch > 1,
                    xb=f"xin{c % 3}", xt=xin[c % 3], ob=f"obuf{c % 2}", ot=obuf[c % 2], pv=1 - (c % 2))

    def stage_b_p1(c):
        bi, ti, tp, odd, xb, xt, ob, ot, pv = (_bctx(c)[k] for k in ("bi", "ti", "tp", "odd", "xb", "xt", "ob", "ot", "pv"))
        for h in range(2):
            if odd:
                S.op("t", lambda e, h=h: e.matmul(pL[:, h * 512:h * 512 + 512], ngones[:], lnw[tp][:, h * 512:h * 512 + 512], start=True, stop=False), reads=["ngones", "pLr", f"lnw{tp}" + ("l" if h == 0 else "h")], writes=["pL"])
                S.op("t", lambda e, h=h: e.matmul(pL[:, h * 512:h * 512 + 512], ngtril[:], lnw[ti][:, h * 512:h * 512 + 512], start=False, stop=True), reads=["ngtril", f"lnw{ti}" + ("l" if h == 0 else "h")], writes=["pL"])
            else:
                S.op("t", lambda e, h=h: e.matmul(pL[:, h * 512:h * 512 + 512], ngtril[:], lnw[ti][:, h * 512:h * 512 + 512], start=True, stop=True), reads=["ngtril", "pLr", f"lnw{ti}" + ("l" if h == 0 else "h")], writes=["pL"])
        S.op("s", lambda e: e.activation(expl[:], pL[:], ACT.Exp), reads=["pL"], writes=["expl"])
        S.op("s", lambda e: e.activation(expnl[:], pL[:], ACT.Exp, scale=-1.0), reads=["pL"], writes=["expnl"])
        for h in range(2):
            u = "l" if h == 0 else "h"
            S.op("v", lambda e, bi=bi, h=h: e.tensor_tensor(kd[bi][:, h * 512:h * 512 + 512], k2[bi][:, h * 512:h * 512 + 512], expnl[:, h * 512:h * 512 + 512], ALU.mult), reads=[f"k2_{bi}{u}", "expnl"], writes=[f"kd{bi}", f"kd{bi}{u}"])

        if odd:
            for h in range(2):
                S.op("t", lambda e, h=h: e.matmul(pL[:, h * 512:h * 512 + 512], ngones[:], lnw[tp][:, h * 512:h * 512 + 512], start=True, stop=False), reads=["ngones", f"lnw{tp}", "expl", "expnl"], writes=["pL"])
                S.op("t", lambda e, h=h: e.matmul(pL[:, h * 512:h * 512 + 512], ngones[:], lnw[ti][:, h * 512:h * 512 + 512], start=False, stop=True), reads=["ngones", f"lnw{ti}" + ("l" if h == 0 else "h")], writes=["pL"])
            S.op("s", lambda e: e.activation(fC[:], pL[:], ACT.Exp), reads=["pL"], writes=["fC", "fCl", "fCh"])

    def stage_b_p2(c):
        bi, ti, tp, odd, xb, xt, ob, ot, pv = (_bctx(c)[k] for k in ("bi", "ti", "tp", "odd", "xb", "xt", "ob", "ot", "pv"))
        for h in range(2):
            for kv in range(4):
                S.op("t", lambda e, h=h, kv=kv, bi=bi: e.matmul(pL[:, h * 512:h * 512 + 128], vt[ti][:, (4 * h + kv) * 128:(4 * h + kv) * 128 + 128], qtb[bi][:, (4 * h + kv) * 128:(4 * h + kv) * 128 + 128], start=(kv == 0), stop=(kv == 3)), reads=[f"vt{ti}", f"qtb{bi}", "fC"], writes=["pL"])
        for h in range(2):
            S.op("v", lambda e, h=h: e.tensor_tensor(gm[:, h * 128:h * 128 + 128], pL[:, h * 512:h * 512 + 128], tmask[:], ALU.mult), reads=["pL", "tmask"], writes=["gm"])
        if odd:
            for h in range(2):
                for kv in range(4):
                    S.op("t", lambda e, h=h, kv=kv: e.matmul(pL[:, h * 512 + 128:h * 512 + 256], vt[tp][:, (4 * h + kv) * 128:(4 * h + kv) * 128 + 128], qtb[bi][:, (4 * h + kv) * 128:(4 * h + kv) * 128 + 128], start=(kv == 0), stop=(kv == 3)), reads=[f"vt{tp}", f"qtb{bi}", "gm"], writes=["pL"])
            for h in range(2):
                S.op("v", lambda e, h=h: e.tensor_copy(gmx[:, h * 128:h * 128 + 128], pL[:, h * 512 + 128:h * 512 + 256]), reads=["pL"], writes=["gmx"])

    def stage_b_p3(c):
        bi, ti, tp, odd, xb, xt, ob, ot, pv = (_bctx(c)[k] for k in ("bi", "ti", "tp", "odd", "xb", "xt", "ob", "ot", "pv"))
        for h in range(2):
            for kv in range(4):
                S.op("t", lambda e, h=h, kv=kv, bi=bi: e.matmul(pY[:, h * 512:h * 512 + 512], qtr[bi][:, (4 * h + kv) * 128:(4 * h + kv) * 128 + 128], St[h][:, kv * 512:kv * 512 + 512], start=(kv == 0), stop=False), reads=[f"qtr{bi}", f"S{h}"], writes=["pY"])
            if odd:
                S.op("t", lambda e, h=h: e.matmul(pY[:, h * 512:h * 512 + 512], gmx[:, h * 128:h * 128 + 128], kd[pv][:, h * 512:h * 512 + 512], start=False, stop=False), reads=["gmx", f"kd{pv}" + ("l" if h == 0 else "h")], writes=["pY"])
            S.op("t", lambda e, h=h, bi=bi: e.matmul(pY[:, h * 512:h * 512 + 512], gm[:, h * 128:h * 128 + 128], kd[bi][:, h * 512:h * 512 + 512], start=False, stop=True), reads=["gm", f"kd{bi}" + ("l" if h == 0 else "h")], writes=["pY"])
        S.op("v", lambda e: e.tensor_tensor(ybuf[:], expl[:], pY[:], ALU.mult), reads=["expl", "pY"], writes=["ybuf"])

        for jq in range(8):
            S.op("t", lambda e, jq=jq: e.matmul(pY[:, jq * 128:(jq + 1) * 128], ybuf[:, jq * 128:(jq + 1) * 128], ident[:], start=True, stop=True), reads=["ybuf", "ident"], writes=["pY"])
        for q in range(4):
            if q % 2 == 0:
                S.op("s", lambda e, q=q: e.copy(yt[:, q * 256:q * 256 + 256], pY[:, q * 256:q * 256 + 256]), reads=["pY"], writes=["yt", "yth" if q == 3 else f"ytq{q}"], cost=0.3)
            else:
                S.op("v", lambda e, q=q: e.tensor_copy(yt[:, q * 256:q * 256 + 256], pY[:, q * 256:q * 256 + 256]), reads=["pY"], writes=["yt", "yth" if q == 3 else f"ytq{q}"], cost=0.3)

    def stage_b_p4(c):
        bi, ti, tp, odd, xb, xt, ob, ot, pv = (_bctx(c)[k] for k in ("bi", "ti", "tp", "odd", "xb", "xt", "ob", "ot", "pv"))
        if odd:
            for h in range(2):
                for kv in range(4):
                    dst, dres = (pL, "pL") if kv < 2 else (pY, "pY")
                    S.op("t", lambda e, h=h, kv=kv, dst=dst: e.matmul(dst[:, (kv % 2) * 512:(kv % 2) * 512 + 512], vtok[tp][:, h * 512 + kv * 128: h * 512 + kv * 128 + 128], kd[pv][:, h * 512:h * 512 + 512], start=True, stop=False), reads=[f"vtok{tp}q{h * 2 + kv // 2}", f"kd{pv}" + ("l" if h == 0 else "h"), "yth" if kv >= 2 else "gmx"], writes=[dres])
                    S.op("t", lambda e, h=h, kv=kv, dst=dst: e.matmul(dst[:, (kv % 2) * 512:(kv % 2) * 512 + 512], vtok[ti][:, h * 512 + kv * 128: h * 512 + kv * 128 + 128], kd[bi][:, h * 512:h * 512 + 512], start=False, stop=True), reads=[f"vtok{ti}q{h * 2 + kv // 2}", f"kd{bi}" + ("l" if h == 0 else "h")], writes=[dres])
                S.op("v", lambda e, h=h: e.tensor_tensor(St[h][:, 0:1024], St[h][:, 0:1024], pL[:], ALU.add), reads=[f"S{h}", "pL"], writes=[f"S{h}"])
                S.op("v", lambda e, h=h: e.tensor_tensor(St[h][:, 1024:2048], St[h][:, 1024:2048], pY[:], ALU.add), reads=[f"S{h}", "pY"], writes=[f"S{h}"])
                for kv in range(4):
                    S.op("v", lambda e, h=h, kv=kv: e.tensor_tensor(St[h][:, kv * 512:kv * 512 + 512], St[h][:, kv * 512:kv * 512 + 512], fC[:, h * 512:h * 512 + 512], ALU.mult), reads=[f"S{h}", "fC"], writes=[f"S{h}"])

        for nh in range(2):
            for jq in range(8):
                S.op("t", lambda e, nh=nh, jq=jq: e.matmul(pL[:, nh * 512:nh * 512 + 512], yt[:, jq * 128:jq * 128 + 128], wot[:, jq * 1024 + nh * 512: jq * 1024 + nh * 512 + 512], start=(jq == 0), stop=(jq == 7)), reads=["yth" if jq // 2 == 3 else f"ytq{jq // 2}", "wot", "pLr"], writes=["pL", f"pLn{nh}"])
        S.op("v", lambda e: e.tensor_tensor(x2[:, 0:512], xt[:, 0:512], pL[:, 0:512], ALU.add), reads=[xb, "pLn0"], writes=["x2", "x2l"])
        S.op("v", lambda e: e.tensor_tensor(x2[:, 512:1024], xt[:, 512:1024], pL[:, 512:1024], ALU.add), reads=[xb, "pLn1"], writes=["x2", "x2h"])
    def stage_b_p5a(c):
        bi, ti, tp, odd, xb, xt, ob, ot, pv = (_bctx(c)[k] for k in ("bi", "ti", "tp", "odd", "xb", "xt", "ob", "ot", "pv"))
        S.op("v", lambda e: e.reduce_sum(smallA[:, 10:11], x2[:, 0:512], AX), reads=["x2l"], writes=["bA"])
        S.op("s", lambda e: e.activation(fB[:, 0:512], x2[:, 0:512], ACT.Square, accum_out=smallA[:, 11:12]), reads=["x2l"], writes=["fB", "fBl", "bB"])
        S.op("v", lambda e: e.reduce_sum(smallA[:, 12:13], x2[:, 512:1024], AX), reads=["x2h"], writes=["bC"])
        S.op("s", lambda e: e.activation(fB[:, 512:1024], x2[:, 512:1024], ACT.Square, accum_out=smallA[:, 13:14]), reads=["x2h"], writes=["fB", "fBh", "bD"])
        S.op("v", lambda e: e.tensor_tensor(smallA[:, 14:15], smallA[:, 10:11], smallA[:, 12:13], ALU.add), reads=["bA", "bC"], writes=["bs0"])
        S.op("v", lambda e: e.tensor_tensor(smallA[:, 15:16], smallA[:, 11:12], smallA[:, 13:14], ALU.add), reads=["bB", "bD"], writes=["bs1"])
        S.op("v", lambda e: e.tensor_scalar(smallA[:, 16:17], smallA[:, 14:15], INV_D, None, ALU.mult), reads=["bs0"], writes=["b0"])
        S.op("v", lambda e: e.tensor_scalar(smallA[:, 17:18], smallA[:, 16:17], smallA[:, 16:17], -EPS, ALU.mult, ALU.add), reads=["b0"], writes=["b1"])
        S.op("v", lambda e: e.scalar_tensor_tensor(smallA[:, 18:19], smallA[:, 15:16], INV_D, smallA[:, 17:18], ALU.mult, ALU.subtract), reads=["bs1", "b1"], writes=["b2"])
        S.op("s", lambda e: e.activation(smallA[:, 19:20], smallA[:, 18:19], ACT.Ln), reads=["b2"], writes=["b3"])
        S.op("s", lambda e: e.activation(smallA[:, 20:21], smallA[:, 19:20], ACT.Exp, scale=-0.5), reads=["b3"], writes=["b4"])
        S.op("v", lambda e: e.scalar_tensor_tensor(smallA[:, 21:22], smallA[:, 16:17], -1.0, smallA[:, 20:21], ALU.mult, ALU.mult), reads=["b0", "b4"], writes=["b5"])
        S.op("s", lambda e: e.activation(x2ln[:, 0:512], x2[:, 0:512], ACT.Identity, bias=smallA[:, 21:22], scale=smallA[:, 20:21]), reads=["x2l", "b4", "b5"], writes=["x2ln", "x2lnl"])
        S.op("s", lambda e: e.activation(x2ln[:, 512:1024], x2[:, 512:1024], ACT.Identity, bias=smallA[:, 21:22], scale=smallA[:, 20:21]), reads=["x2h", "b4", "b5"], writes=["x2ln", "x2lnh"])

    def stage_b_p5b(c):
        bi, ti, tp, odd, xb, xt, ob, ot, pv = (_bctx(c)[k] for k in ("bi", "ti", "tp", "odd", "xb", "xt", "ob", "ot", "pv"))
        for jq in range(8):
            S.op("t", lambda e, jq=jq: e.matmul(pL[:, jq * 128:(jq + 1) * 128], x2ln[:, jq * 128:(jq + 1) * 128], ident[:], start=True, stop=True), reads=["x2lnl" if jq < 4 else "x2lnh", "ident"], writes=["pL", "pLhf0" if jq < 4 else "pLhf1"])
        for u in range(2):
            S.op("v", lambda e, u=u: e.tensor_scalar(xqt[:, u * 512:u * 512 + 512], pL[:, u * 512:u * 512 + 512], scal[:, 2:3], None, ALU.mult), reads=[f"pLhf{u}", "scal2"], writes=["xqt", f"xqth{u}", "pLr"])
        for nh in range(2):
            for j in range(8):
                S.op("t", lambda e, nh=nh, j=j: e.matmul(pY[:, nh * 512:nh * 512 + 512], xqt[:, j * 128:j * 128 + 128], fkq[:, j * 1024 + nh * 512: j * 1024 + nh * 512 + 512], start=(j == 0), stop=(j == 7)), reads=["xqth0" if j < 4 else "xqth1", "fkq"], writes=["pY"])
    def stage_b_p5c(c):
        bi, ti, tp, odd, xb, xt, ob, ot, pv = (_bctx(c)[k] for k in ("bi", "ti", "tp", "odd", "xb", "xt", "ob", "ot", "pv"))
        S.op("v", lambda e: e.tensor_scalar(ybuf[:], pY[:], scal[:, 4:5], None, ALU.mult), reads=["pY", "scal4"], writes=["ybuf"])
        S.op("v", lambda e: e.tensor_tensor(x2ln[:], ybuf[:], biasr[:], ALU.add), reads=["ybuf", "biasr"], writes=["x2ln"])
        S.op("v", lambda e: e.tensor_tensor(ot[:], x2[:], x2ln[:], ALU.add), reads=["x2", "x2l", "x2ln"], writes=[ob])
        S.op("g", lambda e, c=c, ot=ot: e.dma_start(d_out[c * 128:(c + 1) * 128, :], ot[:]), reads=[ob], dma=True)

        if dbg and c == 0:
            for nm, src, res in [("dxln", xln[0], "xln0"), ("dktok", ktok, "ktok"), ("dkhat", khat, "khat"),
                                 ("dlnw", lnw[0], "lnw0"), ("dexpl", expl, "expl"), ("dkd", kd[0], "kd0"),
                                 ("dgm", gm, "gm"), ("dy", ybuf, "ybuf"), ("dx2", x2, "x2"),
                                 ("dxq", xqt, "xqt"), ("dqt", qtr[0], "qtr0")]:
                ap = src[:].bitcast(F32) if src in (lnw[0], qtr[0]) else src[:]
                S.op("sy", lambda e, nm=nm, ap=ap: e.dma_start(dbg_outs[nm], ap), reads=[res], dma=True)
        if dbg and c == 1:
            S.op("sy", lambda e: e.dma_start(dbg_outs["dstate0"], St[0][:].bitcast(F32)), reads=["S0"], dma=True)

    stage_a1(0)
    if nch > 1:
        stage_a1(1)
    stage_a2(0)
    for c in range(nch):
        stage_b_p1(c)
        if c + 1 < nch:
            stage_a2(c + 1, parts=(1,))
        stage_b_p2(c)
        if c + 1 < nch:
            stage_a2(c + 1, parts=(2, 3))
        stage_b_p3(c)
        if c + 1 < nch:
            stage_a2(c + 1, parts=(4,))
        stage_b_p4(c)
        if c + 1 < nch:
            stage_a2(c + 1, parts=(5,))
        stage_b_p5a(c)
        if c + 1 < nch:
            stage_a2(c + 1, parts=(6,))
        stage_b_p5b(c)
        if c + 1 < nch:
            stage_a2(c + 1, parts=(7,))
        if c + 2 < nch:
            stage_a1(c + 2)
        stage_b_p5c(c)

    S.finalize()
    with nc.Block() as block:
        sems = {}
        ctx = []
        for en in ENGINES:
            sem = nc.semaphore(f"sem_{en}")
            ctx.append(sem)
            sems[en] = sem.__enter__()

        @block.sync
        def _(e):
            S.emit_engine("sy", e, sems)

        @block.gpsimd
        def _(e):
            S.emit_engine("g", e, sems)

        @block.tensor
        def _(e):
            S.emit_engine("t", e, sems)

        @block.vector
        def _(e):
            S.emit_engine("v", e, sems)

        @block.scalar
        def _(e):
            S.emit_engine("s", e, sems)

    for t in reversed(sb):
        t.__exit__(None, None, None)
    for sem in reversed(ctx):
        sem.__exit__(None, None, None)
    return nc


def _pack_w(m):
    return np.ascontiguousarray(m.reshape(8, 128, 1024).transpose(1, 0, 2).reshape(128, 8192))


def make_inputs(inputs, T=4096):
    f32 = np.float32
    bf = ml_dtypes.bfloat16
    Wq = np.asarray(inputs["L0_attn_Wq"], dtype=f32)
    Wk = np.asarray(inputs["L0_attn_Wk"], dtype=f32)
    Wv = np.asarray(inputs["L0_attn_Wv"], dtype=f32)
    Wo = np.asarray(inputs["L0_attn_Wo"], dtype=f32)
    fker = np.asarray(inputs["L0_ffn_kernel"])
    if fker.dtype != bf:
        fker = fker.view(bf) if fker.dtype == np.uint16 else fker.astype(bf)
    bias = np.asarray(inputs["L0_ffn_bias"])
    if bias.dtype != bf:
        bias = bias.view(bf) if bias.dtype == np.uint16 else bias.astype(bf)
    wg = np.asarray(inputs["L0_attn_W_gamma"], dtype=f32).reshape(D)
    bg = np.asarray(inputs["L0_attn_b_gamma"], dtype=f32).reshape(D)
    wl = np.asarray(inputs["L0_attn_W_lambda"], dtype=f32).reshape(D)
    bl = np.asarray(inputs["L0_attn_b_lambda"], dtype=f32).reshape(D)
    common = {
        "wqt": _pack_w(np.ascontiguousarray(Wq.T)),
        "wkt": _pack_w(np.ascontiguousarray(Wk.T)),
        "wvt": _pack_w(np.ascontiguousarray(Wv.T)),
        "wot": _pack_w(np.ascontiguousarray(Wo.T)),
        "fk": _pack_w(np.ascontiguousarray(fker)),
        "biasrep": np.ascontiguousarray(np.broadcast_to(bias.reshape(1, D), (128, D))),
        "wgrep": np.ascontiguousarray(np.broadcast_to(wg.reshape(1, D), (128, D))),
        "bgrep": np.ascontiguousarray(np.broadcast_to(bg.reshape(1, D), (128, D))),
        "wlrep": np.ascontiguousarray(np.broadcast_to(wl.reshape(1, D), (128, D))),
        "blrep": np.ascontiguousarray(np.broadcast_to(bl.reshape(1, D), (128, D))),
        "sin": np.asarray(inputs["L0_ffn_input_scale"], dtype=f32).reshape(1, 1),
        "sk": np.asarray(inputs["L0_ffn_kernel_scale"], dtype=f32).reshape(1, 1),
        "negtril": np.ascontiguousarray(-np.triu(np.ones((128, 128), dtype=f32))),
        "negones": np.ascontiguousarray(-np.ones((128, 128), dtype=f32)),
        "trilmask": np.ascontiguousarray(np.triu(np.ones((128, 128))).astype(bf)),
        "ident": np.ascontiguousarray(np.eye(128).astype(bf)),
    }
    return common


def kernel(**inputs):
    x = np.asarray(inputs["L0_attn_x"], dtype=np.float32)
    B, T, Dd = x.shape
    common = make_inputs(inputs, T=T)
    nc = build(T=T)
    in_maps = [dict(common, x=np.ascontiguousarray(x[b])) for b in range(B)]
    res = run_bass_kernel_spmd(nc, in_maps, core_ids=list(range(B)))
    out = np.stack([np.asarray(res.results[b]["out"]) for b in range(B)], axis=0)
    return out
